# version 13
# baseline (speedup 1.0000x reference)
import contextlib
import numpy as np

import concourse.bass as bass
import concourse.mybir as mybir
from concourse.bass_utils import run_bass_kernel_spmd

F32 = mybir.dt.float32
BF16 = mybir.dt.bfloat16
I32 = mybir.dt.int32
AF = mybir.ActivationFunctionType
ALU = mybir.AluOpType
AX = mybir.AxisListType

D = 2048
HID = 5632
NH = 16
DH = 128
TOK = 2048
SEQ = 4096
KT = D // 128
HT = HID // 128
NTT = TOK // 128
TCH = 1024
NCH = TOK // TCH
SHIFT1 = 8.0
EPS_LN = 1e-5
EPS_RMS = 1e-6
QSCALE = float(1.0 / np.sqrt(128.0))
RG = [[0, 1], [2, 3], [4, 5], [6, 7]]

_STAGE = 7

ENGINES = ("tensor", "vector", "scalar", "gpsimd", "sync")


class Walk:

    def __init__(self, nc, who, sems):
        self.nc = nc
        self.who = who
        self.sem_h = sems
        self.cnt = {k: 0 for k in sems}
        self.guard = {}

    def bump(self, ins, sem, k=1):
        if sem is None:
            return None
        self.cnt[sem] += k
        if ins is not None:
            ins.then_inc(self.sem_h[sem], k)
        return self.cnt[sem]

    def wait(self, eng, sem, val=None):
        if self.who == eng:
            v = self.cnt[sem] if val is None else val
            if v > 0:
                getattr(self.nc, eng).wait_ge(self.sem_h[sem], v)

    def mark(self, res, sem):
        self.guard[res] = (sem, self.cnt[sem])

    def req(self, eng, res):
        if res in self.guard:
            sem, c = self.guard[res]
            self.wait(eng, sem, c)

    def mm(self, out, lhsT, rhs, **kw):
        if self.who == "tensor":
            return self.nc.tensor.matmul(out, lhsT, rhs, **kw)

    def tr(self, out, in_, ident):
        if self.who == "tensor":
            return self.nc.tensor.transpose(out, in_, ident)

    def act(self, out, in_, func, **kw):
        if self.who == "scalar":
            if self.cnt["sE"] > 0:
                self.nc.scalar.wait_ge(self.sem_h["sE"], self.cnt["sE"])
            return self.nc.scalar.activation(out, in_, func, **kw)

    def v(self, name, *a, **kw):
        if self.who == "vector":
            if self.cnt["vE"] > 0:
                self.nc.vector.wait_ge(self.sem_h["vE"], self.cnt["vE"])
            return getattr(self.nc.vector, name)(*a, **kw)

    def g(self, name, *a, **kw):
        if self.who == "gpsimd":
            if self.cnt["gE"] > 0:
                self.nc.gpsimd.wait_ge(self.sem_h["gE"], self.cnt["gE"])
            return getattr(self.nc.gpsimd, name)(*a, **kw)

    def dma(self, q, out, in_, **kw):
        if self.who == q:
            return getattr(self.nc, q).dma_start(out, in_, **kw)

    def cc(self, kind, op, rg, ins, outs):
        if self.who == "gpsimd":
            return self.nc.gpsimd.collective_compute(kind, op, rg, ins=ins,
                                                     outs=outs)

    def nop(self):
        return getattr(self.nc, self.who).nop()

    def barrier(self):
        for s in ("dS", "dT", "dQ", "dA", "dO", "dX0", "dX1",
                  "dH0", "dH1"):
            self.wait("sync", s)
        for s in ["dC", "dD0", "dD1", "dG0", "dG1"] + \
                 [f"cc{h}" for h in range(NH)]:
            self.wait("gpsimd", s)
        for e in ENGINES:
            ins = self.nop() if self.who == e else None
            self.bump(ins, "ph", 1)
        for e in ENGINES:
            self.wait(e, "ph")


def build_program(stage=7):
    nc = bass.Bass("TRN2", target_bir_lowering=False)

    x_in = nc.dram_tensor("x_in", [TOK, D], F32, kind="ExternalInput").ap()
    Wq = nc.dram_tensor("Wq", [32, 128, KT * 128], F32,
                        kind="ExternalInput").ap()
    Wk = nc.dram_tensor("Wk", [32, 128, KT * 128], F32,
                        kind="ExternalInput").ap()
    Wv = nc.dram_tensor("Wv", [16, 128, KT * 128], F32,
                        kind="ExternalInput").ap()
    Wo = nc.dram_tensor("Wo", [128, KT * D], F32, kind="ExternalInput").ap()
    nsc = nc.dram_tensor("nsc", [1, D], F32, kind="ExternalInput").ap()
    lam = nc.dram_tensor("lam", [1, NH], F32, kind="ExternalInput").ap()
    wgT = nc.dram_tensor("wgT", [HT, 128, KT * 128], I32,
                         kind="ExternalInput").ap()
    wuT = nc.dram_tensor("wuT", [HT, 128, KT * 128], I32,
                         kind="ExternalInput").ap()
    wdT = nc.dram_tensor("wdT", [8, 128, HT * 256], I32,
                         kind="ExternalInput").ap()
    sg_in = nc.dram_tensor("sg", [128, HT], F32, kind="ExternalInput").ap()
    su_in = nc.dram_tensor("su", [128, HT], F32, kind="ExternalInput").ap()
    sd_in = nc.dram_tensor("sd", [1, D], F32, kind="ExternalInput").ap()

    O = nc.dram_tensor("O", [TOK, D], F32, kind="ExternalOutput").ap()

    x2_dram = nc.dram_tensor("x2_sc", [TOK, D], F32).ap()
    x3_dram = nc.dram_tensor("x3_sc", [TOK, D], F32).ap()
    xnT_dram = [nc.dram_tensor(f"xnT{i}_sc", [D, TOK], BF16).ap()
                for i in range(3)]
    rs_dram = [nc.dram_tensor(f"rs{i}_sc", [TOK, 1], F32).ap()
               for i in range(2)]
    qT_dram = nc.dram_tensor("qT_sc", [2 * DH * NH, TOK], BF16).ap()
    cck_in = nc.dram_tensor("cck_in", [NH * 256, TOK], BF16).ap()
    cck_out = nc.dram_tensor("cck_out", [NH * 512, TOK], BF16).ap()
    ccv_in = nc.dram_tensor("ccv_in", [NH * TOK, DH], BF16).ap()
    ccv_out = nc.dram_tensor("ccv_out", [NH * SEQ, DH], BF16).ap()
    araw_dram = nc.dram_tensor("araw_sc", [TOK, D], BF16).ap()
    row_dram = nc.dram_tensor("row_sc", [4, TOK], F32).ap()

    _used = nc.SBUF_PARTITION_SIZE_BYTES - nc.sbuf_bytes_remaining
    SB0 = (_used + 127) // 128 * 128

    def at(name, shape, dtype, off):
        return nc.alloc_sbuf_tensor_at(name, shape, dtype, offset=SB0 + off)

    ident = at("ident", [128, 128], BF16, 0)
    ones_bf = at("ones_bf", [128, 16], BF16, 256)
    consts = at("consts", [128, 8], F32, 288)
    NS_rep = at("NS_rep", [128, D], BF16, 320)
    Shalf = at("Shalf", [128, D], F32, 4416)
    sg_sb = at("sg_sb", [128, HT], F32, 12608)
    su_sb = at("su_sb", [128, HT], F32, 12800)
    lam_rep = at("lam_rep", [128, NH], F32, 12992)
    oml = at("oml", [128, 1], F32, 13056)
    lam_sb = at("lam_sb", [1, NH], F32, 13088)
    stat = at("stat", [128, 16], F32, 13152)
    lparts = at("lparts", [128, 16], F32, 13216)
    lvals = at("lvals", [128, 8], F32, 13280)
    ca_sb = at("ca_sb", [128, 8], F32, 13312)
    oh_ev = at("oh_ev", [128, 2 * DH], BF16, 13344)
    stat2 = at("stat2", [128, 16], F32, 13856)
    stat3 = at("stat3", [128, 16], F32, 13920)
    stat4 = at("stat4", [128, 16], F32, 13984)
    avals = at("avals", [128, 16], F32, 14048)

    xtile = at("xtile", [128, 2 * D], F32, 14336)
    xsq = at("xsq", [128, D], F32, 30720)
    xnb = at("xnb", [128, 2 * D], BF16, 38912)
    xnt_ev = at("xnt_ev", [128, 2048], BF16, 47104)
    tmp_g = at("tmp_g", [128, 1024], F32, 14336)
    tmp_u = at("tmp_u", [128, 1024], F32, 18432)
    silu_g = at("silu_g", [128, 1024], F32, 22528)
    dn_t = at("dn_t", [128, 1024], F32, 26624)
    dn_t2 = at("dn_t2", [128, 1024], F32, 30720)
    x2out = at("x2out", [128, 1024], F32, 34816)
    xold = at("xold", [128, 1024], F32, 38912)

    AR = 51200
    ln_xt = at("ln_xt", [128, 4 * D], F32, AR)
    ln_xn = at("ln_xn", [128, 4 * D], BF16, AR + 32768)
    ln_sq = at("ln_sq", [128, D], F32, AR + 49152)
    lnev = [at(f"lnev{p}", [128, 2048], BF16, AR + 57344 + p * 4096)
            for p in range(4)]
    xnT_ch = at("xnT_ch", [128, KT * TCH], BF16, AR)
    wg_sb = at("wg_sb", [128, 2 * KT * 128], BF16, 83968)
    wu_sb = at("wu_sb", [128, 2 * KT * 128], BF16, 92160)
    wd_sb = at("wd_sb", [128, HT * 512], BF16, AR)
    act_sl = at("act_sl", [128, HT * TCH], BF16, 100352)
    RS_rep = at("RS_rep", [128, TCH], F32, 190464)
    wqkv_sb = at("wqkv_sb", [128, 2 * KT * 128], BF16, AR)
    qkv_ev = at("qkv_ev", [128, 2 * TOK], BF16, AR + 8192)
    vnat_h = at("vnat_h", [128, TOK], BF16, AR + 16384)
    kv_k1 = at("kv_k1", [128, 2 * SEQ], BF16, AR)
    kv_k2 = at("kv_k2", [128, 2 * SEQ], BF16, 67584)
    kv_v = at("kv_v", [128, 2 * SEQ], BF16, 83968)
    q12 = at("q12", [128, 2 * 2 * TOK], BF16, 100352)
    e1 = at("e1", [128, 2 * SEQ], BF16, 116736)
    e2 = at("e2", [128, 2 * SEQ], BF16, 133120)
    t2b = at("t2b", [128, SEQ], BF16, 149504)
    db = at("db", [128, SEQ], BF16, 157696)
    exp_bf = at("exp_bf", [128, SEQ], BF16, 165888)
    r_bf = at("r_bf", [128, 2 * SEQ], BF16, 182272)
    eT = at("eT", [128, SEQ], BF16, 198656)
    araw_f32 = at("araw_f32", [128, DH], F32, 206848)
    sv_sb = at("sv_sb", [1, DH], F32, 207360)
    csv_bf = at("csv_bf", [1, DH], BF16, 207872)
    ones_row = at("ones_row", [1, DH], BF16, 208384)
    wo_sb = at("wo_sb", [128, KT * D], BF16, AR)
    ar_t = at("ar_t", [128, 2 * D], BF16, 116736)
    on_b = at("on_b", [128, D], BF16, 124928)
    onT = at("onT", [128, 2 * D], BF16, 149504)
    x3t = at("x3t", [128, 2 * D], F32, 133120)

    PS = nc.alloc_psum_tensor("PS", [128, 4096], F32)
    PSb = PS[:].bitcast(BF16)

    def bank(i, n=1):
        return PS[:, i * 512:(i + n) * 512]

    def bankb(i):
        return PSb[:, i * 1024:(i + 1) * 1024]

    ctx = contextlib.ExitStack()
    sems = {}
    names = ["dC", "dS", "dT", "dQ", "dA", "dO", "dX0", "dX1",
             "dG0", "dG1", "dD0", "dD1", "dH0", "dH1",
             "tE", "vE", "sE", "gE", "ph"] + [f"cc{h}" for h in range(NH)]
    for name in names:
        sems[name] = ctx.enter_context(nc.semaphore(name))

    def emit(w: Walk):
        w.bump(w.g("memset", ident[:], 0.0), "gE")
        w.wait("gpsimd", "gE")
        w.bump(w.g("affine_select", out=ident[:], in_=ident[:],
                   compare_op=ALU.not_equal, fill=1.0, base=0,
                   pattern=[[-1, 128]], channel_multiplier=1), "gE")
        w.wait("gpsimd", "gE")
        w.bump(w.g("memset", ones_bf[:], 1.0), "gE")
        w.bump(w.g("memset", consts[:, 0:1], -SHIFT1), "gE")
        w.bump(w.g("memset", consts[:, 1:2], -1.0), "gE")
        w.bump(w.g("memset", consts[:, 2:3], EPS_LN), "gE")
        w.bump(w.g("memset", consts[:, 3:4], EPS_RMS), "gE")
        w.wait("gpsimd", "gE")
        w.bump(w.g("memset", ones_row[:], 1.0), "gE")
        w.bump(w.dma("gpsimd", NS_rep[:], nsc.broadcast_to([128, D])), "dC", 16)
        w.bump(w.dma("gpsimd", Shalf[:], sd_in.broadcast_to([128, D])), "dC", 16)
        w.bump(w.dma("gpsimd", sg_sb[:], sg_in), "dC", 16)
        w.bump(w.dma("gpsimd", su_sb[:], su_in), "dC", 16)
        w.bump(w.dma("gpsimd", lam_rep[:], lam.broadcast_to([128, NH])), "dC", 16)
        w.bump(w.dma("gpsimd", lam_sb[:], lam), "dC", 16)
        w.wait("vector", "dC")
        w.bump(w.v("tensor_scalar", Shalf[:], Shalf[:], 0.5, None, ALU.mult),
               "vE")
        w.bump(w.v("reduce_sum", stat[0:1, 0:1], lam_sb[:], AX.X), "vE")
        w.bump(w.v("tensor_scalar", stat[0:1, 1:2], stat[0:1, 0:1],
                   -1.0 / NH, 1.0, ALU.mult, ALU.add), "vE")
        w.wait("sync", "vE")
        w.bump(w.dma("sync", row_dram[3:4, 0:1], stat[0:1, 1:2]), "dS", 16)
        w.wait("gpsimd", "dS")
        w.bump(w.dma("gpsimd", oml[:],
                     row_dram[3:4, 0:1].broadcast_to([128, 1])), "dC", 16)
        w.barrier()

        def ln_pass(src_ap, dst_xnT, rs_ap, eps_col):
            srcT = src_ap.rearrange("(n p) d -> n p d", p=128)
            dstT3 = dst_xnT.rearrange("(k p) t -> p k t", p=128)
            rsT = (rs_ap.rearrange("(n p) one -> n p one", p=128)
                   if rs_ap is not None else None)
            for i in range(NTT):
                par = i % 4
                st = (stat, stat2, stat3, stat4)[par]
                evb = lnev[par]
                ldq = ("dX0", "dX1", "dS", "dH0")[par]
                stq = ("dT", "dA", "dO", "dQ")[par]
                xt = ln_xt[:, par * D:(par + 1) * D]
                xn = ln_xn[:, par * D:(par + 1) * D]
                w.req("sync", f"xt{par}")
                w.bump(w.dma("sync", xt, srcT[i]), ldq, 16)
                w.req("scalar", f"statv{par}")
                w.wait("scalar", ldq)
                w.bump(w.act(ln_sq[:], xt, AF.Square, accum_out=st[:, 0:1]),
                       "sE")
                w.wait("vector", "sE")
                w.bump(w.v("reduce_sum", st[:, 1:2], xt, AX.X), "vE")
                w.bump(w.v("tensor_scalar", st[:, 2:3], st[:, 1:2],
                           1.0 / D, None, ALU.mult), "vE")
                w.bump(w.v("tensor_scalar", st[:, 3:4], st[:, 0:1],
                           1.0 / D, None, ALU.mult), "vE")
                w.bump(w.v("tensor_tensor", st[:, 4:5], st[:, 2:3],
                           st[:, 2:3], ALU.mult), "vE")
                w.bump(w.v("tensor_tensor", st[:, 5:6], st[:, 3:4],
                           st[:, 4:5], ALU.subtract), "vE")
                w.bump(w.v("tensor_tensor", st[:, 6:7], st[:, 5:6],
                           consts[:, eps_col:eps_col + 1], ALU.add), "vE")
                w.wait("scalar", "vE")
                w.bump(w.act(st[:, 7:8], st[:, 6:7], AF.Ln), "sE")
                w.bump(w.act(st[:, 8:9], st[:, 7:8], AF.Exp, scale=-0.5),
                       "sE")
                w.req("vector", f"xn{par}")
                w.wait("vector", "sE")
                w.bump(w.v("tensor_scalar", xn, xt, st[:, 2:3],
                           st[:, 8:9], ALU.subtract, ALU.mult), "vE")
                if rsT is not None:
                    w.bump(w.v("reduce_sum", st[:, 9:10], xn, AX.X), "vE")
                    w.wait("gpsimd", "vE")
                    w.bump(w.dma("gpsimd", rsT[i], st[:, 9:10]), stq, 16)
                w.mark(f"statv{par}", "vE")
                w.mark(f"xt{par}", "vE")
                w.wait("tensor", "vE")
                for g in range(2):
                    pb = bankb((par * 2 + g) % 8)
                    w.req("tensor", f"lnb{par}{g}")
                    for j in range(8):
                        kk = g * 8 + j
                        w.bump(w.tr(pb[:, j * 128:(j + 1) * 128],
                                    xn[:, kk * 128:(kk + 1) * 128], ident[:]),
                               "tE" if j == 7 else None)
                    w.wait("vector", "tE")
                    w.wait("vector", stq)
                    w.bump(w.v("tensor_copy",
                               evb[:, g * 1024:(g + 1) * 1024], pb), "vE")
                    w.mark(f"lnb{par}{g}", "vE")
                w.mark(f"xn{par}", "tE")
                w.wait("gpsimd", "vE")
                w.bump(w.dma("gpsimd", dstT3[:, :, i * 128:(i + 1) * 128],
                             evb[:, 0:2048]
                             .rearrange("p (k t) -> p k t", t=128)),
                       stq, 16)

        def ffn_pass(xnT_src, rs_ap, xold_src, dst, dst_is_out):
            xoldT = xold_src.rearrange("(n p) d -> n p d", p=128)
            dstT = dst.rearrange("(n p) d -> n p d", p=128)
            osem = "dO" if dst_is_out else "dS"
            for ch in range(NCH):
                t0 = ch * TCH
                w.req("sync", "xnTch")
                w.req("gpsimd", "xnTch")
                xnTt = xnT_src.rearrange("(n p) t -> n p t", p=128)
                for kk in range(KT):
                    w.bump(w.dma("sync", xnT_ch[:, kk * TCH:(kk + 1) * TCH],
                                 xnTt[kk][:, t0:t0 + TCH]), "dS", 16)
                w.req("gpsimd", "rsrep")
                w.bump(w.dma("gpsimd", RS_rep[:],
                             rs_ap[t0:t0 + TCH, 0:1]
                             .rearrange("t one -> one t")
                             .broadcast_to([128, TCH])), "dC", 16)
                w.wait("vector", "dC")
                w.bump(w.v("tensor_scalar", RS_rep[:], RS_rep[:], 128.0,
                           None, ALU.mult), "vE")
                w.wait("tensor", "dS")
                w.wait("tensor", "vE")
                for hh in range(HT):
                    par = hh % 2
                    wg_t = wg_sb[:, par * KT * 128:(par + 1) * KT * 128]
                    wu_t = wu_sb[:, par * KT * 128:(par + 1) * KT * 128]
                    w.req("gpsimd", f"wg{par}")
                    w.bump(w.dma("gpsimd", wg_t, wgT[hh]), f"dG{par}", 16)
                    w.bump(w.dma("gpsimd", wu_t, wuT[hh]), f"dG{par}", 16)
                    w.wait("tensor", f"dG{par}")
                    for nci in range(TCH // 512):
                        bg = bank(nci)
                        bu = bank(2 + nci)
                        w.req("tensor", f"fb{nci}")
                        for kk in range(KT):
                            w.bump(w.mm(bg, wg_t[:, kk * 128:(kk + 1) * 128],
                                        xnT_ch[:, kk * TCH + nci * 512:
                                               kk * TCH + (nci + 1) * 512],
                                        start=(kk == 0), stop=(kk == KT - 1)),
                                   "tE" if kk == KT - 1 else None)
                        for kk in range(KT):
                            w.bump(w.mm(bu, wu_t[:, kk * 128:(kk + 1) * 128],
                                        xnT_ch[:, kk * TCH + nci * 512:
                                               kk * TCH + (nci + 1) * 512],
                                        start=(kk == 0), stop=(kk == KT - 1)),
                                   "tE" if kk == KT - 1 else None)
                        w.mark(f"wg{par}", "tE")
                        sl = slice(nci * 512, (nci + 1) * 512)
                        w.wait("vector", "tE", w.cnt["tE"] - 1)
                        w.req("vector", f"tg{nci}")
                        w.bump(w.v("tensor_tensor", tmp_g[:, sl], bg,
                                   RS_rep[:, sl], ALU.subtract), "vE")
                        w.wait("vector", "tE")
                        w.req("vector", f"tu{nci}")
                        w.bump(w.v("tensor_tensor", tmp_u[:, sl], bu,
                                   RS_rep[:, sl], ALU.subtract), "vE")
                        w.mark(f"fb{nci}", "vE")
                        w.wait("scalar", "vE", w.cnt["vE"] - 1)
                        w.bump(w.act(silu_g[:, sl], tmp_g[:, sl], AF.Silu,
                                     scale=sg_sb[:, hh:hh + 1]), "sE")
                        w.mark(f"tg{nci}", "sE")
                        w.wait("vector", "sE")
                        w.bump(w.v("scalar_tensor_tensor",
                                   act_sl[:, hh * TCH + nci * 512:
                                          hh * TCH + (nci + 1) * 512],
                                   tmp_u[:, sl], su_sb[:, hh:hh + 1],
                                   silu_g[:, sl], ALU.mult, ALU.mult), "vE")
                        w.mark(f"tu{nci}", "vE")
                w.wait("tensor", "vE")
                for m in range(TCH // 128):
                    bca = bank(6 + m % 2)[:, 0:1]
                    w.req("tensor", f"ca{m % 2}")
                    for kk in range(HT):
                        w.bump(w.mm(bca,
                                    act_sl[:, kk * TCH + m * 128:
                                           kk * TCH + (m + 1) * 128],
                                    ones_bf[:, 0:1],
                                    start=(kk == 0), stop=(kk == HT - 1)),
                               "tE" if kk == HT - 1 else None)
                    w.wait("vector", "tE")
                    w.bump(w.v("tensor_scalar", ca_sb[:, m:m + 1], bca,
                               128.0, None, ALU.mult), "vE")
                    w.mark(f"ca{m % 2}", "vE")
                HB = HT * 256
                for dc in range(8):
                    wb = dc % 2
                    wd_h = wd_sb[:, wb * HB:(wb + 1) * HB]
                    if dc == 0:
                        w.wait("gpsimd", "tE")
                        w.bump(w.dma("gpsimd", wd_sb[:, 0:HB], wdT[0]),
                               "dD0", 16)
                        w.bump(w.dma("gpsimd", wd_sb[:, HB:2 * HB], wdT[1]),
                               "dD1", 16)
                    elif dc < 7:
                        nb = (dc + 1) % 2
                        w.req("gpsimd", f"wdfree{nb}")
                        w.bump(w.dma("gpsimd",
                                     wd_sb[:, nb * HB:(nb + 1) * HB],
                                     wdT[dc + 1]), f"dD{nb}", 16)
                    w.wait("tensor", f"dD{wb}")
                    for m in range(TCH // 128):
                        par = m % 2
                        bd = bank(4 + par)[:, 0:256]
                        w.req("tensor", f"db{par}")
                        for kk in range(HT):
                            w.bump(w.mm(bd,
                                        act_sl[:, kk * TCH + m * 128:
                                               kk * TCH + (m + 1) * 128],
                                        wd_h[:, kk * 256:(kk + 1) * 256],
                                        start=(kk == 0), stop=(kk == HT - 1)),
                                   "tE" if kk == HT - 1 else None)
                        if m == TCH // 128 - 1:
                            w.mark(f"wdfree{wb}", "tE")
                        tsl = slice(par * 512, par * 512 + 256)
                        w.req("sync", f"xo{par}")
                        w.bump(w.dma("sync", xold[:, tsl],
                                     xoldT[t0 // 128 + m]
                                     [:, dc * 256:(dc + 1) * 256]),
                               f"dX{par}", 16)
                        w.wait("vector", "tE")
                        w.wait("vector", f"dX{par}")
                        w.req("vector", f"dnt{par}")
                        w.bump(w.v("tensor_scalar", dn_t[:, tsl], bd,
                                   ca_sb[:, m:m + 1], None, ALU.subtract),
                               "vE")
                        w.mark(f"db{par}", "vE")
                        w.bump(w.v("tensor_tensor", dn_t2[:, tsl],
                                   dn_t[:, tsl],
                                   Shalf[:, dc * 256:(dc + 1) * 256],
                                   ALU.mult), "vE")
                        w.wait("vector", osem)
                        w.bump(w.v("tensor_tensor", x2out[:, tsl],
                                   dn_t2[:, tsl], xold[:, tsl], ALU.add),
                               "vE")
                        w.mark(f"xo{par}", "vE")
                        w.mark(f"dnt{par}", "vE")
                        w.wait("scalar", "vE")
                        w.bump(w.dma("scalar",
                                     dstT[t0 // 128 + m]
                                     [:, dc * 256:(dc + 1) * 256],
                                     x2out[:, tsl]), osem, 16)
                w.mark("xnTch", "tE")
                w.mark("rsrep", "vE")

        ln_pass(x_in, xnT_dram[0], rs_dram[0], 2)
        w.barrier()
        ffn_pass(xnT_dram[0], rs_dram[0], x_in,
                 O if stage == 2 else x2_dram, stage == 2)
        w.barrier()
        if stage == 2:
            w.wait("sync", "dO")
            return

        ln_pass(x2_dram, xnT_dram[1], None, 2)
        w.barrier()

        xn2Tt = xnT_dram[1].rearrange("(n p) t -> n p t", p=128)
        xr = act_sl
        for kk in range(KT):
            w.bump(w.dma("sync", xr[:, kk * TOK:(kk + 1) * TOK], xn2Tt[kk]),
                   "dS", 16)
        w.wait("tensor", "dS")
        qTt = qT_dram.rearrange("(n p) t -> n p t", p=128)
        cck_in_t = cck_in.rearrange("(n p) t -> n p t", p=128)
        ccv_in_t = ccv_in.rearrange("(n p) d -> n p d", p=128)

        def proj(Wsrc, ot, dst_tile, scale):
            par = ot % 2
            wt = wqkv_sb[:, par * KT * 128:(par + 1) * KT * 128]
            w.req("gpsimd", f"pw{par}")
            w.bump(w.dma("gpsimd", wt, Wsrc[ot]), f"dG{par}", 16)
            w.wait("tensor", f"dG{par}")
            ev = qkv_ev[:, par * TOK:(par + 1) * TOK]
            for nci in range(TOK // 512):
                bq = bank(nci)
                w.req("tensor", f"pb{nci}")
                for kk in range(KT):
                    w.bump(w.mm(bq, wt[:, kk * 128:(kk + 1) * 128],
                                xr[:, kk * TOK + nci * 512:
                                   kk * TOK + (nci + 1) * 512],
                                start=(kk == 0), stop=(kk == KT - 1)),
                           "tE" if kk == KT - 1 else None)
                sl = slice(nci * 512, (nci + 1) * 512)
                w.wait("vector", "tE")
                w.wait("vector", "dQ")
                w.req("vector", f"pev{par}")
                if scale is not None:
                    w.bump(w.v("tensor_scalar", ev[:, sl], bq, scale, None,
                               ALU.mult), "vE")
                else:
                    w.bump(w.v("tensor_copy", ev[:, sl], bq), "vE")
                w.mark(f"pb{nci}", "vE")
            w.mark(f"pw{par}", "tE")
            if dst_tile is not None:
                w.wait("sync", "vE")
                w.bump(w.dma("sync", dst_tile, ev), "dQ", 16)
            return ev

        for ot in range(32):
            proj(Wq, ot, qTt[ot], QSCALE)
        for ot in range(32):
            h, comp = ot // 2, ot % 2
            proj(Wk, ot, cck_in_t[h * 2 + comp], None)
            if comp == 1:
                w.wait("gpsimd", "dQ")
                w.bump(w.cc("AllGather", ALU.bypass, RG,
                            [cck_in[h * 256:(h + 1) * 256, :]],
                            [cck_out[h * 512:(h + 1) * 512, :]]), f"cc{h}")
        for h in range(NH):
            ev = proj(Wv, h, None, None)
            par = h % 2
            w.wait("tensor", "vE")
            for g in range(2):
                pb = bankb(6 + g)
                w.req("tensor", f"vt{g}")
                for j in range(8):
                    tt = g * 8 + j
                    w.bump(w.tr(pb[:, j * 128:(j + 1) * 128],
                                ev[:, tt * 128:(tt + 1) * 128], ident[:]),
                           "tE" if j == 7 else None)
                w.wait("vector", "tE")
                w.wait("vector", "dQ")
                w.bump(w.v("tensor_copy", vnat_h[:, g * 1024:(g + 1) * 1024],
                           pb), "vE")
                w.mark(f"vt{g}", "vE")
            w.mark(f"pev{par}", "tE")
            w.wait("sync", "vE")
            for tt in range(NTT):
                w.bump(w.dma("sync", ccv_in_t[h * NTT + tt],
                             vnat_h[:, tt * 128:(tt + 1) * 128]), "dQ", 16)
            w.wait("gpsimd", "dQ")
            w.bump(w.cc("AllGather", ALU.bypass, RG,
                        [ccv_in[h * TOK:(h + 1) * TOK, :]],
                        [ccv_out[h * SEQ:(h + 1) * SEQ, :]]), f"cc{h}")
        w.barrier()

        cck_out_t = cck_out.rearrange("(n p) t -> n p t", p=128)
        arawT = araw_dram.rearrange("(n p) d -> n p d", p=128)
        OT = O.rearrange("(n p) d -> n p d", p=128)

        def head_dma(h):
            par = h % 2
            base = h * 4
            w.wait("sync", f"cc{h}", 2)
            w.req("sync", f"hd{par}")
            for half in range(2):
                w.bump(w.dma("sync",
                             kv_k1[:, par * SEQ + half * TOK:
                                   par * SEQ + (half + 1) * TOK],
                             cck_out_t[base + 2 * half]), f"dH{par}", 16)
                w.bump(w.dma("sync",
                             kv_k2[:, par * SEQ + half * TOK:
                                   par * SEQ + (half + 1) * TOK],
                             cck_out_t[base + 2 * half + 1]), f"dH{par}", 16)
            w.bump(w.dma("sync", kv_v[:, par * SEQ:(par + 1) * SEQ],
                         ccv_out[h * SEQ:(h + 1) * SEQ, :]
                         .rearrange("(t p) d -> p t d", p=128)),
                   f"dH{par}", 16)
            w.bump(w.dma("sync", q12[:, par * 2 * TOK:par * 2 * TOK + TOK],
                         qTt[h * 2]), f"dH{par}", 16)
            w.bump(w.dma("sync",
                         q12[:, par * 2 * TOK + TOK:(par + 1) * 2 * TOK],
                         qTt[h * 2 + 1]), f"dH{par}", 16)

        head_dma(0)
        for h in range(NH):
            par = h % 2
            k1 = kv_k1[:, par * SEQ:(par + 1) * SEQ]
            k2 = kv_k2[:, par * SEQ:(par + 1) * SEQ]
            vn = kv_v[:, par * SEQ:(par + 1) * SEQ]
            q1 = q12[:, par * 2 * TOK:par * 2 * TOK + TOK]
            q2 = q12[:, par * 2 * TOK + TOK:(par + 1) * 2 * TOK]
            w.wait("tensor", f"dH{par}")
            if h + 1 < NH:
                head_dma(h + 1)
            w.req("tensor", "tb0")
            for kt in range(32):
                w.bump(w.mm(bank(4)[0:1, 0:DH], ones_bf[:, 0:1],
                            vn[:, kt * 128:(kt + 1) * 128],
                            start=(kt == 0), stop=(kt == 31)),
                       "tE" if kt == 31 else None)
            w.wait("vector", "tE")
            w.req("vector", "csv")
            w.bump(w.v("tensor_scalar", csv_bf[:], bank(4)[0:1, 0:DH],
                       float(np.exp(-1.0)), None, ALU.mult), "vE")
            w.mark("tb0", "vE")
            def stage_scores(qb):
                up = qb % 2
                e1b = e1[:, up * SEQ:(up + 1) * SEQ]
                e2b = e2[:, up * SEQ:(up + 1) * SEQ]
                for sc_i in range(2):
                    qh = q1 if sc_i == 0 else q2
                    kh = k1 if sc_i == 0 else k2
                    ebuf = e1b if sc_i == 0 else e2b
                    lp0 = up * 8 + sc_i * 4
                    for g in range(4):
                        gb = bank(2 * (g % 2), 2)
                        w.req("tensor", f"sg{g % 2}")
                        for half in range(2):
                            w.bump(w.mm(gb[:, half * 512:(half + 1) * 512],
                                        qh[:, qb * 128:(qb + 1) * 128],
                                        kh[:, g * 1024 + half * 512:
                                           g * 1024 + (half + 1) * 512],
                                        start=True, stop=True),
                                   "tE" if half == 1 else None)
                        w.wait("scalar", "tE")
                        w.req("scalar", f"e1buf{up}" if sc_i == 0
                              else f"e2buf{up}")
                        w.bump(w.act(ebuf[:, g * 1024:(g + 1) * 1024], gb,
                                     AF.Exp, bias=consts[:, 0:1],
                                     accum_out=lparts[:, lp0 + g:
                                                      lp0 + g + 1]), "sE")
                        w.mark(f"sg{g % 2}", "sE")
                w.mark(f"expdone{qb % 2}", "sE")

            def stage_elem(qb):
                up = qb % 2
                e1b = e1[:, up * SEQ:(up + 1) * SEQ]
                e2b = e2[:, up * SEQ:(up + 1) * SEQ]
                lp0 = up * 8
                av = lambda c: avals[:, up * 8 + c:up * 8 + c + 1]
                w.req("vector", f"expdone{up}")
                w.bump(w.v("reduce_sum", av(0),
                           lparts[:, lp0:lp0 + 4], AX.X), "vE")
                w.bump(w.v("reduce_sum", av(1),
                           lparts[:, lp0 + 4:lp0 + 8], AX.X), "vE")
                w.bump(w.v("reciprocal", av(2), av(0)), "vE")
                w.bump(w.v("reciprocal", av(3), av(1)), "vE")
                w.bump(w.v("tensor_tensor", av(4), av(3),
                           lam_rep[:, h:h + 1], ALU.mult), "vE")
                w.bump(w.v("tensor_tensor", av(4), av(4), av(0),
                           ALU.mult), "vE")
                w.req("vector", "t2buf")
                w.bump(w.v("tensor_scalar", t2b[:], e2b, av(4),
                           None, ALU.mult), "vE")
                w.mark(f"e2buf{up}", "vE")
                w.req("vector", "dbuf")
                w.bump(w.v("tensor_tensor", db[:], e1b, t2b[:],
                           ALU.subtract), "vE")
                w.mark(f"e1buf{up}", "vE")
                w.mark("t2buf", "vE")
                w.wait("scalar", "vE")
                w.req("scalar", "expf")
                w.req("scalar", "zrd")
                w.bump(w.act(exp_bf[:], db[:], AF.Exp, scale=av(2),
                             bias=consts[:, 1:2],
                             accum_out=lvals[:, 5:6]), "sE")
                w.mark("dbuf", "sE")

            def stage_elem_b(qb):
                up = qb % 2
                rbb = r_bf[:, up * SEQ:(up + 1) * SEQ]
                w.wait("vector", "sE")
                w.req("vector", f"edbuf{up}")
                w.bump(w.v("tensor_scalar", rbb, exp_bf[:],
                           float(-np.exp(-1.0)), None, ALU.add), "vE")
                w.mark("expf", "vE")
                w.mark(f"rrdy{up}", "vE")
                w.bump(w.v("reciprocal", lvals[:, 6 + up:7 + up],
                           lvals[:, 5:6]), "vE")
                w.mark("zrd", "vE")

            def stage_tpv(qb):
                up = qb % 2
                rbb = r_bf[:, up * SEQ:(up + 1) * SEQ]
                pvb = bank(6 + up)[:, 0:128]
                w.req("tensor", f"rrdy{up}")
                eT_done = []
                for g in range(4):
                    pb = bankb(4 + (g % 2))
                    w.req("tensor", f"tb{g % 2}")
                    for j in range(8):
                        tt = g * 8 + j
                        w.bump(w.tr(pb[:, j * 128:(j + 1) * 128],
                                    rbb[:, tt * 128:(tt + 1) * 128],
                                    ident[:]), "tE" if j == 7 else None)
                    w.wait("vector", "tE")
                    w.req("vector", f"eTg{g}")
                    w.bump(w.v("tensor_copy", eT[:, g * 1024:(g + 1) * 1024],
                               pb), "vE")
                    w.mark(f"tb{g % 2}", "vE")
                    eT_done.append(w.cnt["vE"])
                w.mark(f"edbuf{up}", "tE")
                w.req("tensor", f"pv{up}")
                for j in range(32):
                    if j % 8 == 0:
                        w.wait("tensor", "vE", eT_done[j // 8])
                    w.bump(w.mm(pvb, eT[:, j * 128:(j + 1) * 128],
                                vn[:, j * 128:(j + 1) * 128],
                                start=(j == 0), stop=False), None)
                w.bump(w.mm(pvb, ones_row[:], csv_bf[:],
                            start=False, stop=True), "tE")
                for g in range(4):
                    w.mark(f"eTg{g}", "tE")
                w.wait("vector", "tE")
                ohsl = oh_ev[:, up * DH:(up + 1) * DH]
                w.wait("vector", "dA")
                w.bump(w.v("tensor_scalar", ohsl, pvb,
                           lvals[:, 6 + up:7 + up], None, ALU.mult), "vE")
                w.mark(f"pv{up}", "vE")
                w.wait("sync", "vE")
                w.bump(w.dma("sync", arawT[qb][:, h * 128:(h + 1) * 128],
                             ohsl), "dA", 16)
                if stage == 5:
                    w.wait("vector", "dO")
                    w.bump(w.v("tensor_copy", araw_f32[:], ohsl), "vE")
                    w.wait("sync", "vE")
                    w.bump(w.dma("sync", OT[qb][:, h * 128:(h + 1) * 128],
                                 araw_f32[:]), "dO", 16)

            for qb in range(NTT):
                stage_scores(qb)
                if qb > 0:
                    stage_elem(qb - 1)
                if qb > 1:
                    stage_tpv(qb - 2)
                if qb > 0:
                    stage_elem_b(qb - 1)
            stage_elem(NTT - 1)
            stage_tpv(NTT - 2)
            stage_elem_b(NTT - 1)
            stage_tpv(NTT - 1)
            w.mark(f"hd{par}", "tE")
            w.mark("csv", "tE")
        w.barrier()
        if stage == 5:
            w.wait("sync", "dO")
            return

        for kk4 in range(4):
            w.bump(w.dma("gpsimd", wo_sb[:, kk4 * 4 * D:(kk4 + 1) * 4 * D],
                         Wo[:, kk4 * 4 * D:(kk4 + 1) * 4 * D]), "dC", 16)
        x2T = x2_dram.rearrange("(n p) d -> n p d", p=128)
        x3T = x3_dram.rearrange("(n p) d -> n p d", p=128)
        w.wait("tensor", "dC")
        for i in range(NTT):
            par = i % 2
            st = stat if par == 0 else stat2
            stq6 = "dA" if par == 0 else "dT"
            art = ar_t[:, par * D:(par + 1) * D]
            x3tl = x3t[:, par * D:(par + 1) * D]
            xt = xtile[:, par * D:(par + 1) * D]
            w.req("sync", f"art{par}")
            w.bump(w.dma("sync", art, arawT[i]), f"dX{par}", 16)
            w.req("sync", f"x2t{par}")
            w.bump(w.dma("sync", xt, x2T[i]), f"dX{par}", 16)
            w.req("scalar", f"statv{par}")
            w.wait("scalar", f"dX{par}")
            w.bump(w.act(xsq[:], art, AF.Square, accum_out=st[:, 0:1]),
                   "sE")
            w.wait("vector", "sE")
            w.bump(w.v("tensor_scalar", st[:, 1:2], st[:, 0:1], 1.0 / D,
                       None, ALU.mult), "vE")
            w.bump(w.v("tensor_tensor", st[:, 2:3], st[:, 1:2],
                       consts[:, 3:4], ALU.add), "vE")
            w.wait("scalar", "vE")
            w.bump(w.act(st[:, 3:4], st[:, 2:3], AF.Ln), "sE")
            w.bump(w.act(st[:, 4:5], st[:, 3:4], AF.Exp, scale=-0.5),
                   "sE")
            w.wait("vector", "sE")
            w.bump(w.v("tensor_tensor", st[:, 5:6], st[:, 4:5], oml[:],
                       ALU.mult), "vE")
            w.mark(f"statv{par}", "vE")
            w.req("vector", "onb")
            w.bump(w.v("tensor_scalar", on_b[:], art, st[:, 5:6], None,
                       ALU.mult), "vE")
            w.mark(f"art{par}", "vE")
            w.bump(w.v("tensor_tensor", on_b[:], on_b[:], NS_rep[:],
                       ALU.mult), "vE")
            onTp = onT[:, par * D:(par + 1) * D]
            w.wait("tensor", "vE")
            w.req("tensor", f"onT{par}")
            for g in range(2):
                pb = bankb(g)
                w.req("tensor", f"wtb{g}")
                for j in range(8):
                    kk = g * 8 + j
                    w.bump(w.tr(pb[:, j * 128:(j + 1) * 128],
                                on_b[:, kk * 128:(kk + 1) * 128], ident[:]),
                           "tE" if j == 7 else None)
                w.wait("vector", "tE")
                w.bump(w.v("tensor_copy", onTp[:, g * 1024:(g + 1) * 1024],
                           pb), "vE")
                w.mark(f"wtb{g}", "vE")
            w.mark("onb", "tE")
            w.wait("tensor", "vE")
            for nci in range(4):
                bq = bank(2 + nci % 2)
                w.req("tensor", f"wob{nci % 2}")
                for kk in range(KT):
                    w.bump(w.mm(bq, onTp[:, kk * 128:(kk + 1) * 128],
                                wo_sb[:, kk * D + nci * 512:
                                      kk * D + (nci + 1) * 512],
                                start=(kk == 0), stop=(kk == KT - 1)),
                           "tE" if kk == KT - 1 else None)
                if nci == 3:
                    w.mark(f"onT{par}", "tE")
                w.wait("vector", "tE")
                sl = slice(nci * 512, (nci + 1) * 512)
                w.wait("vector", f"dX{par}")
                w.wait("vector", stq6)
                w.bump(w.v("tensor_tensor", x3tl[:, sl], bq, xt[:, sl],
                           ALU.add), "vE")
                w.mark(f"wob{nci % 2}", "vE")
                if nci == 3:
                    w.mark(f"x2t{par}", "vE")
            w.wait("gpsimd", "vE")
            w.bump(w.dma("gpsimd", x3T[i], x3tl), stq6, 16)
        w.barrier()
        if stage == 6:
            for i in range(NTT):
                par = i % 2
                w.wait("sync", "dO")
                w.bump(w.dma("sync", xtile[:, par * D:(par + 1) * D],
                             x3T[i]), "dS", 16)
                w.wait("sync", "dS")
                w.bump(w.dma("sync", OT[i], xtile[:, par * D:(par + 1) * D]),
                       "dO", 16)
            w.wait("sync", "dO")
            return

        ln_pass(x3_dram, xnT_dram[2], rs_dram[1], 2)
        w.barrier()
        ffn_pass(xnT_dram[2], rs_dram[1], x3_dram, O, True)
        w.wait("sync", "dO")
        w.barrier()

    with nc.Block() as block:
        walks = {}

        def run_walk(eng):
            wk = Walk(nc, eng, sems)
            walks[eng] = wk
            emit(wk)

        @block.tensor
        def _(te):
            run_walk("tensor")

        @block.vector
        def _(ve):
            run_walk("vector")

        @block.scalar
        def _(sc):
            run_walk("scalar")

        @block.gpsimd
        def _(gp):
            run_walk("gpsimd")

        @block.sync
        def _(sy):
            run_walk("sync")

    base = walks["tensor"].cnt
    for eng, wk in walks.items():
        assert wk.cnt == base, (eng, wk.cnt, base)
    ctx.close()
    return nc


def _tile_hid_d(w):
    return np.ascontiguousarray(
        w.reshape(HT, 128, KT, 128).transpose(0, 3, 2, 1)
        .reshape(HT, 128, KT * 128))


def _tile_din_dout(w):
    o = w.shape[1]
    return np.ascontiguousarray(
        w.reshape(KT, 128, o // 128, 128).transpose(2, 1, 0, 3)
        .reshape(o // 128, 128, KT * 128))


def _prep_inputs(L0_attn_x, L0_attn_Wq, L0_attn_Wk, L0_attn_Wv, L0_attn_Wo,
                 L0_attn_norm_scale, L0_attn_lam, L0_ffn_wq_gate, L0_ffn_s_gate,
                 L0_ffn_wq_up, L0_ffn_s_up, L0_ffn_wq_down, L0_ffn_s_down):
    x = np.ascontiguousarray(np.asarray(L0_attn_x, dtype=np.float32))
    common = dict(
        Wq=_tile_din_dout(np.asarray(L0_attn_Wq, dtype=np.float32)),
        Wk=_tile_din_dout(np.asarray(L0_attn_Wk, dtype=np.float32)),
        Wv=_tile_din_dout(np.asarray(L0_attn_Wv, dtype=np.float32)),
        Wo=np.ascontiguousarray(
            np.asarray(L0_attn_Wo, dtype=np.float32)
            .reshape(KT, 128, D).transpose(1, 0, 2).reshape(128, KT * D)),
        nsc=np.ascontiguousarray(
            np.asarray(L0_attn_norm_scale, dtype=np.float32).reshape(1, D)),
        lam=np.ascontiguousarray(
            np.asarray(L0_attn_lam, dtype=np.float32).reshape(1, NH)),
        wgT=_tile_hid_d(np.asarray(L0_ffn_wq_gate, dtype=np.int32)),
        wuT=_tile_hid_d(np.asarray(L0_ffn_wq_up, dtype=np.int32)),
        wdT=np.ascontiguousarray(
            np.asarray(L0_ffn_wq_down, dtype=np.int32)
            .reshape(8, 256, HT, 128).transpose(0, 3, 2, 1)
            .reshape(8, 128, HT * 256)),
        sg=np.ascontiguousarray(
            np.asarray(L0_ffn_s_gate, dtype=np.float32).reshape(HT, 128).T),
        su=np.ascontiguousarray(
            np.asarray(L0_ffn_s_up, dtype=np.float32).reshape(HT, 128).T),
        sd=np.ascontiguousarray(
            np.asarray(L0_ffn_s_down, dtype=np.float32).reshape(1, D)),
    )
    in_maps = []
    for c in range(8):
        b, half = c // 2, c % 2
        m = dict(common)
        m["x_in"] = np.ascontiguousarray(x[b, half * TOK:(half + 1) * TOK, :])
        in_maps.append(m)
    return in_maps


def kernel(**inputs):
    in_maps = _prep_inputs(**inputs)
    nc = build_program(stage=_STAGE)
    res = run_bass_kernel_spmd(nc, in_maps, core_ids=list(range(8)))
    out = np.zeros((4, SEQ, D), dtype=np.float32)
    for c in range(8):
        b, half = c // 2, c % 2
        out[b, half * TOK:(half + 1) * TOK, :] = np.asarray(
            res.results[c]["O"], dtype=np.float32)
    return out


# version 14
# speedup vs baseline: 1.2110x; 1.2110x over previous
import contextlib
import numpy as np

import concourse.bass as bass
import concourse.mybir as mybir
from concourse.bass_utils import run_bass_kernel_spmd

F32 = mybir.dt.float32
BF16 = mybir.dt.bfloat16
I32 = mybir.dt.int32
AF = mybir.ActivationFunctionType
ALU = mybir.AluOpType
AX = mybir.AxisListType

D = 2048
HID = 5632
NH = 16
DH = 128
TOK = 2048
SEQ = 4096
KT = D // 128
HT = HID // 128
NTT = TOK // 128
TCH = 1024
NCH = TOK // TCH
SHIFT1 = 8.0
EPS_LN = 1e-5
EPS_RMS = 1e-6
QSCALE = float(1.0 / np.sqrt(128.0))
RG = [[0, 1], [2, 3], [4, 5], [6, 7]]

_STAGE = 7

ENGINES = ("tensor", "vector", "scalar", "gpsimd", "sync")


class Walk:

    def __init__(self, nc, who, sems):
        self.nc = nc
        self.who = who
        self.sem_h = sems
        self.cnt = {k: 0 for k in sems}
        self.guard = {}

    def bump(self, ins, sem, k=1):
        if sem is None:
            return None
        self.cnt[sem] += k
        if ins is not None:
            ins.then_inc(self.sem_h[sem], k)
        return self.cnt[sem]

    def wait(self, eng, sem, val=None):
        if self.who == eng:
            v = self.cnt[sem] if val is None else val
            if v > 0:
                getattr(self.nc, eng).wait_ge(self.sem_h[sem], v)

    def mark(self, res, sem):
        self.guard[res] = (sem, self.cnt[sem])

    def req(self, eng, res):
        if res in self.guard:
            sem, c = self.guard[res]
            self.wait(eng, sem, c)

    def mm(self, out, lhsT, rhs, **kw):
        if self.who == "tensor":
            return self.nc.tensor.matmul(out, lhsT, rhs, **kw)

    def tr(self, out, in_, ident):
        if self.who == "tensor":
            return self.nc.tensor.transpose(out, in_, ident)

    def act(self, out, in_, func, **kw):
        if self.who == "scalar":
            if self.cnt["sE"] > 0:
                self.nc.scalar.wait_ge(self.sem_h["sE"], self.cnt["sE"])
            return self.nc.scalar.activation(out, in_, func, **kw)

    def v(self, name, *a, **kw):
        if self.who == "vector":
            if self.cnt["vE"] > 0:
                self.nc.vector.wait_ge(self.sem_h["vE"], self.cnt["vE"])
            return getattr(self.nc.vector, name)(*a, **kw)

    def g(self, name, *a, **kw):
        if self.who == "gpsimd":
            if self.cnt["gE"] > 0:
                self.nc.gpsimd.wait_ge(self.sem_h["gE"], self.cnt["gE"])
            return getattr(self.nc.gpsimd, name)(*a, **kw)

    def dma(self, q, out, in_, **kw):
        if self.who == q:
            return getattr(self.nc, q).dma_start(out, in_, **kw)

    def cc(self, kind, op, rg, ins, outs):
        if self.who == "gpsimd":
            return self.nc.gpsimd.collective_compute(kind, op, rg, ins=ins,
                                                     outs=outs)

    def nop(self):
        return getattr(self.nc, self.who).nop()

    def barrier(self):
        for s in ("dS", "dT", "dQ", "dA", "dO", "dX0", "dX1",
                  "dH0", "dH1"):
            self.wait("sync", s)
        for s in ["dC", "dD0", "dD1", "dG0", "dG1"] + \
                 [f"cc{h}" for h in range(NH)]:
            self.wait("gpsimd", s)
        for e in ENGINES:
            ins = self.nop() if self.who == e else None
            self.bump(ins, "ph", 1)
        for e in ENGINES:
            self.wait(e, "ph")


def build_program(stage=7):
    nc = bass.Bass("TRN2", target_bir_lowering=False)

    x_in = nc.dram_tensor("x_in", [TOK, D], F32, kind="ExternalInput").ap()
    Wq = nc.dram_tensor("Wq", [32, 128, KT * 128], F32,
                        kind="ExternalInput").ap()
    Wk = nc.dram_tensor("Wk", [32, 128, KT * 128], F32,
                        kind="ExternalInput").ap()
    Wv = nc.dram_tensor("Wv", [16, 128, KT * 128], F32,
                        kind="ExternalInput").ap()
    Wo = nc.dram_tensor("Wo", [128, KT * D], F32, kind="ExternalInput").ap()
    nsc = nc.dram_tensor("nsc", [1, D], F32, kind="ExternalInput").ap()
    lam = nc.dram_tensor("lam", [1, NH], F32, kind="ExternalInput").ap()
    wgT = nc.dram_tensor("wgT", [HT, 128, KT * 128], I32,
                         kind="ExternalInput").ap()
    wuT = nc.dram_tensor("wuT", [HT, 128, KT * 128], I32,
                         kind="ExternalInput").ap()
    wdT = nc.dram_tensor("wdT", [8, 128, HT * 256], I32,
                         kind="ExternalInput").ap()
    sg_in = nc.dram_tensor("sg", [128, HT], F32, kind="ExternalInput").ap()
    su_in = nc.dram_tensor("su", [128, HT], F32, kind="ExternalInput").ap()
    sd_in = nc.dram_tensor("sd", [1, D], F32, kind="ExternalInput").ap()

    O = nc.dram_tensor("O", [TOK, D], F32, kind="ExternalOutput").ap()

    x2_dram = nc.dram_tensor("x2_sc", [TOK, D], F32).ap()
    x3_dram = nc.dram_tensor("x3_sc", [TOK, D], F32).ap()
    xnT_dram = [nc.dram_tensor(f"xnT{i}_sc", [D, TOK], BF16).ap()
                for i in range(3)]
    rs_dram = [nc.dram_tensor(f"rs{i}_sc", [TOK, 1], F32).ap()
               for i in range(2)]
    qT_dram = nc.dram_tensor("qT_sc", [2 * DH * NH, TOK], BF16).ap()
    cck_in = nc.dram_tensor("cck_in", [NH * 256, TOK], BF16).ap()
    cck_out = nc.dram_tensor("cck_out", [NH * 512, TOK], BF16).ap()
    ccv_in = nc.dram_tensor("ccv_in", [NH * TOK, DH], BF16).ap()
    ccv_out = nc.dram_tensor("ccv_out", [NH * SEQ, DH], BF16).ap()
    araw_dram = nc.dram_tensor("araw_sc", [TOK, D], BF16).ap()
    row_dram = nc.dram_tensor("row_sc", [4, TOK], F32).ap()

    _used = nc.SBUF_PARTITION_SIZE_BYTES - nc.sbuf_bytes_remaining
    SB0 = (_used + 127) // 128 * 128

    def at(name, shape, dtype, off):
        return nc.alloc_sbuf_tensor_at(name, shape, dtype, offset=SB0 + off)

    ident = at("ident", [128, 128], BF16, 0)
    ones_bf = at("ones_bf", [128, 16], BF16, 256)
    consts = at("consts", [128, 8], F32, 288)
    NS_rep = at("NS_rep", [128, D], BF16, 320)
    Shalf = at("Shalf", [128, D], F32, 4416)
    sg_sb = at("sg_sb", [128, HT], F32, 12608)
    su_sb = at("su_sb", [128, HT], F32, 12800)
    lam_rep = at("lam_rep", [128, NH], F32, 12992)
    oml = at("oml", [128, 1], F32, 13056)
    lam_sb = at("lam_sb", [1, NH], F32, 13088)
    stat = at("stat", [128, 16], F32, 13152)
    lparts = at("lparts", [128, 16], F32, 13216)
    lvals = at("lvals", [128, 8], F32, 13280)
    ca_sb = at("ca_sb", [128, 8], F32, 13312)
    oh_ev = at("oh_ev", [128, 2 * DH], BF16, 13344)
    stat2 = at("stat2", [128, 16], F32, 13856)
    stat3 = at("stat3", [128, 16], F32, 13920)
    stat4 = at("stat4", [128, 16], F32, 13984)
    avals = at("avals", [128, 16], F32, 14048)

    xtile = at("xtile", [128, 2 * D], F32, 14336)
    xsq = at("xsq", [128, D], F32, 30720)
    xnb = at("xnb", [128, 2 * D], BF16, 38912)
    xnt_ev = at("xnt_ev", [128, 2048], BF16, 47104)
    tmp_g = at("tmp_g", [128, 1024], F32, 14336)
    tmp_u = at("tmp_u", [128, 1024], F32, 18432)
    silu_g = at("silu_g", [128, 1024], F32, 22528)
    dn_t = at("dn_t", [128, 1024], F32, 26624)
    dn_t2 = at("dn_t2", [128, 1024], F32, 30720)
    x2out = at("x2out", [128, 1024], F32, 34816)
    xold = at("xold", [128, 1024], F32, 38912)

    AR = 51200
    ln_xt = at("ln_xt", [128, 4 * D], F32, AR)
    ln_xn = at("ln_xn", [128, 4 * D], BF16, AR + 32768)
    ln_sq = at("ln_sq", [128, D], F32, AR + 49152)
    lnev = [at(f"lnev{p}", [128, 2048], BF16, AR + 57344 + p * 4096)
            for p in range(4)]
    xnT_ch = at("xnT_ch", [128, KT * TCH], BF16, AR)
    wg_sb = at("wg_sb", [128, 2 * KT * 128], BF16, 83968)
    wu_sb = at("wu_sb", [128, 2 * KT * 128], BF16, 92160)
    wd_sb = at("wd_sb", [128, HT * 512], BF16, AR)
    act_sl = at("act_sl", [128, HT * TCH], BF16, 100352)
    RS_rep = at("RS_rep", [128, TCH], F32, 190464)
    wqkv_sb = at("wqkv_sb", [128, 2 * KT * 128], BF16, AR)
    qkv_ev = at("qkv_ev", [128, 2 * TOK], BF16, AR + 8192)
    vnat_h = at("vnat_h", [128, TOK], BF16, AR + 16384)
    kv_k1 = at("kv_k1", [128, 2 * SEQ], BF16, AR)
    kv_k2 = at("kv_k2", [128, 2 * SEQ], BF16, 67584)
    kv_v = at("kv_v", [128, 2 * SEQ], BF16, 83968)
    q12 = at("q12", [128, 2 * 2 * TOK], BF16, 100352)
    e1 = at("e1", [128, 2 * SEQ], BF16, 116736)
    e2 = at("e2", [128, 2 * SEQ], BF16, 133120)
    t2b = at("t2b", [128, SEQ], BF16, 149504)
    db = at("db", [128, SEQ], BF16, 157696)
    exp_bf = at("exp_bf", [128, SEQ], BF16, 165888)
    r_bf = at("r_bf", [128, 2 * SEQ], BF16, 182272)
    eT = at("eT", [128, SEQ], BF16, 198656)
    araw_f32 = at("araw_f32", [128, DH], F32, 206848)
    sv_sb = at("sv_sb", [1, DH], F32, 207360)
    csv_bf = at("csv_bf", [1, DH], BF16, 207872)
    ones_row = at("ones_row", [1, DH], BF16, 208384)
    wo_sb = at("wo_sb", [128, KT * D], BF16, AR)
    ar_t = at("ar_t", [128, 2 * D], BF16, 116736)
    on_b = at("on_b", [128, D], BF16, 124928)
    onT = at("onT", [128, 2 * D], BF16, 149504)
    x3t = at("x3t", [128, 2 * D], F32, 133120)

    PS = nc.alloc_psum_tensor("PS", [128, 4096], F32)
    PSb = PS[:].bitcast(BF16)

    def bank(i, n=1):
        return PS[:, i * 512:(i + n) * 512]

    def bankb(i):
        return PSb[:, i * 1024:(i + 1) * 1024]

    ctx = contextlib.ExitStack()
    sems = {}
    names = ["dC", "dS", "dT", "dQ", "dA", "dO", "dX0", "dX1",
             "dG0", "dG1", "dD0", "dD1", "dH0", "dH1",
             "tE", "vE", "sE", "gE", "ph"] + [f"cc{h}" for h in range(NH)]
    for name in names:
        sems[name] = ctx.enter_context(nc.semaphore(name))

    def emit(w: Walk):
        w.bump(w.g("memset", ident[:], 0.0), "gE")
        w.wait("gpsimd", "gE")
        w.bump(w.g("affine_select", out=ident[:], in_=ident[:],
                   compare_op=ALU.not_equal, fill=1.0, base=0,
                   pattern=[[-1, 128]], channel_multiplier=1), "gE")
        w.wait("gpsimd", "gE")
        w.bump(w.g("memset", ones_bf[:], 1.0), "gE")
        w.bump(w.g("memset", consts[:, 0:1], -SHIFT1), "gE")
        w.bump(w.g("memset", consts[:, 1:2], -1.0), "gE")
        w.bump(w.g("memset", consts[:, 2:3], EPS_LN), "gE")
        w.bump(w.g("memset", consts[:, 3:4], EPS_RMS), "gE")
        w.wait("gpsimd", "gE")
        w.bump(w.g("memset", ones_row[:], 1.0), "gE")
        w.bump(w.dma("gpsimd", NS_rep[:], nsc.broadcast_to([128, D])), "dC", 16)
        w.bump(w.dma("gpsimd", Shalf[:], sd_in.broadcast_to([128, D])), "dC", 16)
        w.bump(w.dma("gpsimd", sg_sb[:], sg_in), "dC", 16)
        w.bump(w.dma("gpsimd", su_sb[:], su_in), "dC", 16)
        w.bump(w.dma("gpsimd", lam_rep[:], lam.broadcast_to([128, NH])), "dC", 16)
        w.bump(w.dma("gpsimd", lam_sb[:], lam), "dC", 16)
        w.wait("vector", "dC")
        w.bump(w.v("tensor_scalar", Shalf[:], Shalf[:], 0.5, None, ALU.mult),
               "vE")
        w.bump(w.v("reduce_sum", stat[0:1, 0:1], lam_sb[:], AX.X), "vE")
        w.bump(w.v("tensor_scalar", stat[0:1, 1:2], stat[0:1, 0:1],
                   -1.0 / NH, 1.0, ALU.mult, ALU.add), "vE")
        w.wait("sync", "vE")
        w.bump(w.dma("sync", row_dram[3:4, 0:1], stat[0:1, 1:2]), "dS", 16)
        w.wait("gpsimd", "dS")
        w.bump(w.dma("gpsimd", oml[:],
                     row_dram[3:4, 0:1].broadcast_to([128, 1])), "dC", 16)
        w.barrier()

        def ln_pass(src_ap, dst_xnT, rs_ap, eps_col):
            srcT = src_ap.rearrange("(n p) d -> n p d", p=128)
            dstT3 = dst_xnT.rearrange("(k p) t -> p k t", p=128)
            rsT = (rs_ap.rearrange("(n p) one -> n p one", p=128)
                   if rs_ap is not None else None)
            for i in range(NTT):
                par = i % 4
                st = (stat, stat2, stat3, stat4)[par]
                evb = lnev[par]
                ldq = ("dX0", "dX1", "dS", "dH0")[par]
                stq = ("dT", "dA", "dO", "dQ")[par]
                xt = ln_xt[:, par * D:(par + 1) * D]
                xn = ln_xn[:, par * D:(par + 1) * D]
                w.req("sync", f"xt{par}")
                w.bump(w.dma("sync", xt, srcT[i]), ldq, 16)
                w.req("scalar", f"statv{par}")
                w.wait("scalar", ldq)
                w.bump(w.act(ln_sq[:], xt, AF.Square, accum_out=st[:, 0:1]),
                       "sE")
                w.wait("vector", "sE")
                w.bump(w.v("reduce_sum", st[:, 1:2], xt, AX.X), "vE")
                w.bump(w.v("tensor_scalar", st[:, 2:3], st[:, 1:2],
                           1.0 / D, None, ALU.mult), "vE")
                w.bump(w.v("tensor_scalar", st[:, 3:4], st[:, 0:1],
                           1.0 / D, None, ALU.mult), "vE")
                w.bump(w.v("tensor_tensor", st[:, 4:5], st[:, 2:3],
                           st[:, 2:3], ALU.mult), "vE")
                w.bump(w.v("tensor_tensor", st[:, 5:6], st[:, 3:4],
                           st[:, 4:5], ALU.subtract), "vE")
                w.bump(w.v("tensor_tensor", st[:, 6:7], st[:, 5:6],
                           consts[:, eps_col:eps_col + 1], ALU.add), "vE")
                w.wait("scalar", "vE")
                w.bump(w.act(st[:, 7:8], st[:, 6:7], AF.Ln), "sE")
                w.bump(w.act(st[:, 8:9], st[:, 7:8], AF.Exp, scale=-0.5),
                       "sE")
                w.req("vector", f"xn{par}")
                w.wait("vector", "sE")
                w.bump(w.v("tensor_scalar", xn, xt, st[:, 2:3],
                           st[:, 8:9], ALU.subtract, ALU.mult), "vE")
                if rsT is not None:
                    w.bump(w.v("reduce_sum", st[:, 9:10], xn, AX.X), "vE")
                    w.wait("gpsimd", "vE")
                    w.bump(w.dma("gpsimd", rsT[i], st[:, 9:10]), stq, 16)
                w.mark(f"statv{par}", "vE")
                w.mark(f"xt{par}", "vE")
                w.wait("tensor", "vE")
                for g in range(2):
                    pb = bankb((par * 2 + g) % 8)
                    w.req("tensor", f"lnb{par}{g}")
                    for j in range(8):
                        kk = g * 8 + j
                        w.bump(w.tr(pb[:, j * 128:(j + 1) * 128],
                                    xn[:, kk * 128:(kk + 1) * 128], ident[:]),
                               "tE" if j == 7 else None)
                    w.wait("vector", "tE")
                    w.wait("vector", stq)
                    w.bump(w.v("tensor_copy",
                               evb[:, g * 1024:(g + 1) * 1024], pb), "vE")
                    w.mark(f"lnb{par}{g}", "vE")
                w.mark(f"xn{par}", "tE")
                w.wait("gpsimd", "vE")
                w.bump(w.dma("gpsimd", dstT3[:, :, i * 128:(i + 1) * 128],
                             evb[:, 0:2048]
                             .rearrange("p (k t) -> p k t", t=128)),
                       stq, 16)

        def ffn_pass(xnT_src, rs_ap, xold_src, dst, dst_is_out):
            xoldT = xold_src.rearrange("(n p) d -> n p d", p=128)
            dstT = dst.rearrange("(n p) d -> n p d", p=128)
            osem = "dO" if dst_is_out else "dS"
            for ch in range(NCH):
                t0 = ch * TCH
                w.req("sync", "xnTch")
                w.req("gpsimd", "xnTch")
                xnTt = xnT_src.rearrange("(n p) t -> n p t", p=128)
                for kk in range(KT):
                    w.bump(w.dma("sync", xnT_ch[:, kk * TCH:(kk + 1) * TCH],
                                 xnTt[kk][:, t0:t0 + TCH]), "dS", 16)
                w.req("gpsimd", "rsrep")
                w.bump(w.dma("gpsimd", RS_rep[:],
                             rs_ap[t0:t0 + TCH, 0:1]
                             .rearrange("t one -> one t")
                             .broadcast_to([128, TCH])), "dC", 16)
                w.wait("vector", "dC")
                w.bump(w.v("tensor_scalar", RS_rep[:], RS_rep[:], 128.0,
                           None, ALU.mult), "vE")
                w.wait("tensor", "dS")
                w.wait("tensor", "vE")
                for hh in range(HT):
                    par = hh % 2
                    wg_t = wg_sb[:, par * KT * 128:(par + 1) * KT * 128]
                    wu_t = wu_sb[:, par * KT * 128:(par + 1) * KT * 128]
                    w.req("gpsimd", f"wg{par}")
                    w.bump(w.dma("gpsimd", wg_t, wgT[hh]), f"dG{par}", 16)
                    w.bump(w.dma("gpsimd", wu_t, wuT[hh]), f"dG{par}", 16)
                    w.wait("tensor", f"dG{par}")
                    for nci in range(TCH // 512):
                        bg = bank(nci)
                        bu = bank(2 + nci)
                        w.req("tensor", f"fb{nci}")
                        for kk in range(KT):
                            w.bump(w.mm(bg, wg_t[:, kk * 128:(kk + 1) * 128],
                                        xnT_ch[:, kk * TCH + nci * 512:
                                               kk * TCH + (nci + 1) * 512],
                                        start=(kk == 0), stop=(kk == KT - 1)),
                                   "tE" if kk == KT - 1 else None)
                        for kk in range(KT):
                            w.bump(w.mm(bu, wu_t[:, kk * 128:(kk + 1) * 128],
                                        xnT_ch[:, kk * TCH + nci * 512:
                                               kk * TCH + (nci + 1) * 512],
                                        start=(kk == 0), stop=(kk == KT - 1)),
                                   "tE" if kk == KT - 1 else None)
                        w.mark(f"wg{par}", "tE")
                        sl = slice(nci * 512, (nci + 1) * 512)
                        w.wait("vector", "tE", w.cnt["tE"] - 1)
                        w.req("vector", f"tg{nci}")
                        w.bump(w.v("tensor_tensor", tmp_g[:, sl], bg,
                                   RS_rep[:, sl], ALU.subtract), "vE")
                        w.wait("vector", "tE")
                        w.req("vector", f"tu{nci}")
                        w.bump(w.v("tensor_tensor", tmp_u[:, sl], bu,
                                   RS_rep[:, sl], ALU.subtract), "vE")
                        w.mark(f"fb{nci}", "vE")
                        w.wait("scalar", "vE", w.cnt["vE"] - 1)
                        w.bump(w.act(silu_g[:, sl], tmp_g[:, sl], AF.Silu,
                                     scale=sg_sb[:, hh:hh + 1]), "sE")
                        w.mark(f"tg{nci}", "sE")
                        w.wait("vector", "sE")
                        w.bump(w.v("scalar_tensor_tensor",
                                   act_sl[:, hh * TCH + nci * 512:
                                          hh * TCH + (nci + 1) * 512],
                                   tmp_u[:, sl], su_sb[:, hh:hh + 1],
                                   silu_g[:, sl], ALU.mult, ALU.mult), "vE")
                        w.mark(f"tu{nci}", "vE")
                w.wait("tensor", "vE")
                for m in range(TCH // 128):
                    bca = bank(6 + m % 2)[:, 0:1]
                    w.req("tensor", f"ca{m % 2}")
                    for kk in range(HT):
                        w.bump(w.mm(bca,
                                    act_sl[:, kk * TCH + m * 128:
                                           kk * TCH + (m + 1) * 128],
                                    ones_bf[:, 0:1],
                                    start=(kk == 0), stop=(kk == HT - 1)),
                               "tE" if kk == HT - 1 else None)
                    w.wait("vector", "tE")
                    w.bump(w.v("tensor_scalar", ca_sb[:, m:m + 1], bca,
                               128.0, None, ALU.mult), "vE")
                    w.mark(f"ca{m % 2}", "vE")
                HB = HT * 256
                for dc in range(8):
                    wb = dc % 2
                    wd_h = wd_sb[:, wb * HB:(wb + 1) * HB]
                    if dc == 0:
                        w.wait("gpsimd", "tE")
                        w.bump(w.dma("gpsimd", wd_sb[:, 0:HB], wdT[0]),
                               "dD0", 16)
                        w.bump(w.dma("gpsimd", wd_sb[:, HB:2 * HB], wdT[1]),
                               "dD1", 16)
                    elif dc < 7:
                        nb = (dc + 1) % 2
                        w.req("gpsimd", f"wdfree{nb}")
                        w.bump(w.dma("gpsimd",
                                     wd_sb[:, nb * HB:(nb + 1) * HB],
                                     wdT[dc + 1]), f"dD{nb}", 16)
                    w.wait("tensor", f"dD{wb}")
                    for m in range(TCH // 128):
                        par = m % 2
                        bd = bank(4 + par)[:, 0:256]
                        w.req("tensor", f"db{par}")
                        for kk in range(HT):
                            w.bump(w.mm(bd,
                                        act_sl[:, kk * TCH + m * 128:
                                               kk * TCH + (m + 1) * 128],
                                        wd_h[:, kk * 256:(kk + 1) * 256],
                                        start=(kk == 0), stop=(kk == HT - 1)),
                                   "tE" if kk == HT - 1 else None)
                        if m == TCH // 128 - 1:
                            w.mark(f"wdfree{wb}", "tE")
                        tsl = slice(par * 512, par * 512 + 256)
                        w.req("sync", f"xo{par}")
                        w.bump(w.dma("sync", xold[:, tsl],
                                     xoldT[t0 // 128 + m]
                                     [:, dc * 256:(dc + 1) * 256]),
                               f"dX{par}", 16)
                        w.wait("vector", "tE")
                        w.wait("vector", f"dX{par}")
                        w.req("vector", f"dnt{par}")
                        w.bump(w.v("tensor_scalar", dn_t[:, tsl], bd,
                                   ca_sb[:, m:m + 1], None, ALU.subtract),
                               "vE")
                        w.mark(f"db{par}", "vE")
                        w.bump(w.v("tensor_tensor", dn_t2[:, tsl],
                                   dn_t[:, tsl],
                                   Shalf[:, dc * 256:(dc + 1) * 256],
                                   ALU.mult), "vE")
                        w.wait("vector", osem)
                        w.bump(w.v("tensor_tensor", x2out[:, tsl],
                                   dn_t2[:, tsl], xold[:, tsl], ALU.add),
                               "vE")
                        w.mark(f"xo{par}", "vE")
                        w.mark(f"dnt{par}", "vE")
                        w.wait("scalar", "vE")
                        w.bump(w.dma("scalar",
                                     dstT[t0 // 128 + m]
                                     [:, dc * 256:(dc + 1) * 256],
                                     x2out[:, tsl]), osem, 16)
                w.mark("xnTch", "tE")
                w.mark("rsrep", "vE")

        ln_pass(x_in, xnT_dram[0], rs_dram[0], 2)
        w.barrier()
        ffn_pass(xnT_dram[0], rs_dram[0], x_in,
                 O if stage == 2 else x2_dram, stage == 2)
        w.barrier()
        if stage == 2:
            w.wait("sync", "dO")
            return

        ln_pass(x2_dram, xnT_dram[1], None, 2)
        w.barrier()

        xn2Tt = xnT_dram[1].rearrange("(n p) t -> n p t", p=128)
        xr = act_sl
        for kk in range(KT):
            w.bump(w.dma("sync", xr[:, kk * TOK:(kk + 1) * TOK], xn2Tt[kk]),
                   "dS", 16)
        w.wait("tensor", "dS")
        qTt = qT_dram.rearrange("(n p) t -> n p t", p=128)
        cck_in_t = cck_in.rearrange("(n p) t -> n p t", p=128)
        ccv_in_t = ccv_in.rearrange("(n p) d -> n p d", p=128)

        pw_cnt = {}

        def proj_load(Wsrc, ot):
            par = ot % 2
            wt = wqkv_sb[:, par * KT * 128:(par + 1) * KT * 128]
            w.req("gpsimd", f"pw{par}")
            w.bump(w.dma("gpsimd", wt, Wsrc[ot]), f"dG{par}", 16)
            pw_cnt[ot] = w.cnt[f"dG{par}"]

        def proj(ot, dst_tile, scale):
            par = ot % 2
            wt = wqkv_sb[:, par * KT * 128:(par + 1) * KT * 128]
            w.wait("tensor", f"dG{par}", pw_cnt[ot])
            ev = qkv_ev[:, par * TOK:(par + 1) * TOK]
            for nci in range(TOK // 512):
                bq = bank(nci)
                w.req("tensor", f"pb{nci}")
                for kk in range(KT):
                    w.bump(w.mm(bq, wt[:, kk * 128:(kk + 1) * 128],
                                xr[:, kk * TOK + nci * 512:
                                   kk * TOK + (nci + 1) * 512],
                                start=(kk == 0), stop=(kk == KT - 1)),
                           "tE" if kk == KT - 1 else None)
                sl = slice(nci * 512, (nci + 1) * 512)
                w.wait("vector", "tE")
                w.wait("vector", "dQ")
                w.req("vector", f"pev{par}")
                if scale is not None:
                    w.bump(w.v("tensor_scalar", ev[:, sl], bq, scale, None,
                               ALU.mult), "vE")
                else:
                    w.bump(w.v("tensor_copy", ev[:, sl], bq), "vE")
                w.mark(f"pb{nci}", "vE")
            w.mark(f"pw{par}", "tE")
            if dst_tile is not None:
                w.wait("sync", "vE")
                w.bump(w.dma("sync", dst_tile, ev), "dQ", 16)
            return ev

        proj_load(Wq, 0)
        for ot in range(32):
            if ot + 1 < 32:
                proj_load(Wq, ot + 1)
            proj(ot, qTt[ot], QSCALE)
        proj_load(Wk, 0)
        for ot in range(32):
            h, comp = ot // 2, ot % 2
            if ot + 1 < 32:
                proj_load(Wk, ot + 1)
            proj(ot, cck_in_t[h * 2 + comp], None)
            if comp == 1:
                w.wait("gpsimd", "dQ")
                w.bump(w.cc("AllGather", ALU.bypass, RG,
                            [cck_in[h * 256:(h + 1) * 256, :]],
                            [cck_out[h * 512:(h + 1) * 512, :]]), f"cc{h}")
        proj_load(Wv, 0)
        for h in range(NH):
            if h + 1 < NH:
                proj_load(Wv, h + 1)
            ev = proj(h, None, None)
            par = h % 2
            w.wait("tensor", "vE")
            for g in range(2):
                pb = bankb(6 + g)
                w.req("tensor", f"vt{g}")
                for j in range(8):
                    tt = g * 8 + j
                    w.bump(w.tr(pb[:, j * 128:(j + 1) * 128],
                                ev[:, tt * 128:(tt + 1) * 128], ident[:]),
                           "tE" if j == 7 else None)
                w.wait("vector", "tE")
                w.wait("vector", "dQ")
                w.bump(w.v("tensor_copy", vnat_h[:, g * 1024:(g + 1) * 1024],
                           pb), "vE")
                w.mark(f"vt{g}", "vE")
            w.mark(f"pev{par}", "tE")
            w.wait("sync", "vE")
            for tt in range(NTT):
                w.bump(w.dma("sync", ccv_in_t[h * NTT + tt],
                             vnat_h[:, tt * 128:(tt + 1) * 128]), "dQ", 16)
            w.wait("gpsimd", "dQ")
            w.bump(w.cc("AllGather", ALU.bypass, RG,
                        [ccv_in[h * TOK:(h + 1) * TOK, :]],
                        [ccv_out[h * SEQ:(h + 1) * SEQ, :]]), f"cc{h}")
        w.barrier()

        cck_out_t = cck_out.rearrange("(n p) t -> n p t", p=128)
        arawT = araw_dram.rearrange("(n p) d -> n p d", p=128)
        OT = O.rearrange("(n p) d -> n p d", p=128)

        def head_dma(h):
            par = h % 2
            base = h * 4
            w.wait("sync", f"cc{h}", 2)
            w.req("sync", f"hd{par}")
            for half in range(2):
                w.bump(w.dma("sync",
                             kv_k1[:, par * SEQ + half * TOK:
                                   par * SEQ + (half + 1) * TOK],
                             cck_out_t[base + 2 * half]), f"dH{par}", 16)
                w.bump(w.dma("sync",
                             kv_k2[:, par * SEQ + half * TOK:
                                   par * SEQ + (half + 1) * TOK],
                             cck_out_t[base + 2 * half + 1]), f"dH{par}", 16)
            w.bump(w.dma("sync", kv_v[:, par * SEQ:(par + 1) * SEQ],
                         ccv_out[h * SEQ:(h + 1) * SEQ, :]
                         .rearrange("(t p) d -> p t d", p=128)),
                   f"dH{par}", 16)
            w.bump(w.dma("sync", q12[:, par * 2 * TOK:par * 2 * TOK + TOK],
                         qTt[h * 2]), f"dH{par}", 16)
            w.bump(w.dma("sync",
                         q12[:, par * 2 * TOK + TOK:(par + 1) * 2 * TOK],
                         qTt[h * 2 + 1]), f"dH{par}", 16)

        head_dma(0)
        for h in range(NH):
            par = h % 2
            k1 = kv_k1[:, par * SEQ:(par + 1) * SEQ]
            k2 = kv_k2[:, par * SEQ:(par + 1) * SEQ]
            vn = kv_v[:, par * SEQ:(par + 1) * SEQ]
            q1 = q12[:, par * 2 * TOK:par * 2 * TOK + TOK]
            q2 = q12[:, par * 2 * TOK + TOK:(par + 1) * 2 * TOK]
            w.wait("tensor", f"dH{par}")
            if h + 1 < NH:
                head_dma(h + 1)
            w.req("tensor", "tb0")
            for kt in range(32):
                w.bump(w.mm(bank(4)[0:1, 0:DH], ones_bf[:, 0:1],
                            vn[:, kt * 128:(kt + 1) * 128],
                            start=(kt == 0), stop=(kt == 31)),
                       "tE" if kt == 31 else None)
            w.wait("vector", "tE")
            w.req("vector", "csv")
            w.bump(w.v("tensor_scalar", csv_bf[:], bank(4)[0:1, 0:DH],
                       float(np.exp(-1.0)), None, ALU.mult), "vE")
            w.mark("tb0", "vE")
            def stage_scores(qb):
                up = qb % 2
                e1b = e1[:, up * SEQ:(up + 1) * SEQ]
                e2b = e2[:, up * SEQ:(up + 1) * SEQ]
                for sc_i in range(2):
                    qh = q1 if sc_i == 0 else q2
                    kh = k1 if sc_i == 0 else k2
                    ebuf = e1b if sc_i == 0 else e2b
                    lp0 = up * 8 + sc_i * 4
                    for g in range(4):
                        gb = bank(2 * (g % 2), 2)
                        w.req("tensor", f"sg{g % 2}")
                        for half in range(2):
                            w.bump(w.mm(gb[:, half * 512:(half + 1) * 512],
                                        qh[:, qb * 128:(qb + 1) * 128],
                                        kh[:, g * 1024 + half * 512:
                                           g * 1024 + (half + 1) * 512],
                                        start=True, stop=True),
                                   "tE" if half == 1 else None)
                        w.wait("scalar", "tE")
                        w.req("scalar", f"e1buf{up}" if sc_i == 0
                              else f"e2buf{up}")
                        w.bump(w.act(ebuf[:, g * 1024:(g + 1) * 1024], gb,
                                     AF.Exp, bias=consts[:, 0:1],
                                     accum_out=lparts[:, lp0 + g:
                                                      lp0 + g + 1]), "sE")
                        w.mark(f"sg{g % 2}", "sE")
                w.mark(f"expdone{qb % 2}", "sE")

            def stage_elem(qb):
                up = qb % 2
                e1b = e1[:, up * SEQ:(up + 1) * SEQ]
                e2b = e2[:, up * SEQ:(up + 1) * SEQ]
                lp0 = up * 8
                av = lambda c: avals[:, up * 8 + c:up * 8 + c + 1]
                w.req("vector", f"expdone{up}")
                w.bump(w.v("reduce_sum", av(0),
                           lparts[:, lp0:lp0 + 4], AX.X), "vE")
                w.bump(w.v("reduce_sum", av(1),
                           lparts[:, lp0 + 4:lp0 + 8], AX.X), "vE")
                w.bump(w.v("reciprocal", av(2), av(0)), "vE")
                w.bump(w.v("reciprocal", av(3), av(1)), "vE")
                w.bump(w.v("tensor_tensor", av(4), av(3),
                           lam_rep[:, h:h + 1], ALU.mult), "vE")
                w.bump(w.v("tensor_tensor", av(4), av(4), av(0),
                           ALU.mult), "vE")
                w.req("vector", "t2buf")
                w.bump(w.v("tensor_scalar", t2b[:], e2b, av(4),
                           None, ALU.mult), "vE")
                w.mark(f"e2buf{up}", "vE")
                w.req("vector", "dbuf")
                w.bump(w.v("tensor_tensor", db[:], e1b, t2b[:],
                           ALU.subtract), "vE")
                w.mark(f"e1buf{up}", "vE")
                w.mark("t2buf", "vE")
                w.wait("scalar", "vE")
                w.req("scalar", "expf")
                w.req("scalar", "zrd")
                w.bump(w.act(exp_bf[:], db[:], AF.Exp, scale=av(2),
                             bias=consts[:, 1:2],
                             accum_out=lvals[:, 5:6]), "sE")
                w.mark("dbuf", "sE")

            def stage_elem_b(qb):
                up = qb % 2
                rbb = r_bf[:, up * SEQ:(up + 1) * SEQ]
                w.wait("vector", "sE")
                w.req("vector", f"edbuf{up}")
                w.bump(w.v("tensor_scalar", rbb, exp_bf[:],
                           float(-np.exp(-1.0)), None, ALU.add), "vE")
                w.mark("expf", "vE")
                w.mark(f"rrdy{up}", "vE")
                w.bump(w.v("reciprocal", lvals[:, 6 + up:7 + up],
                           lvals[:, 5:6]), "vE")
                w.mark("zrd", "vE")

            def stage_tpv(qb):
                up = qb % 2
                rbb = r_bf[:, up * SEQ:(up + 1) * SEQ]
                pvb = bank(6 + up)[:, 0:128]
                w.req("tensor", f"rrdy{up}")
                eT_done = []
                for g in range(4):
                    pb = bankb(4 + (g % 2))
                    w.req("tensor", f"tb{g % 2}")
                    for j in range(8):
                        tt = g * 8 + j
                        w.bump(w.tr(pb[:, j * 128:(j + 1) * 128],
                                    rbb[:, tt * 128:(tt + 1) * 128],
                                    ident[:]), "tE" if j == 7 else None)
                    w.wait("vector", "tE")
                    w.req("vector", f"eTg{g}")
                    w.bump(w.v("tensor_copy", eT[:, g * 1024:(g + 1) * 1024],
                               pb), "vE")
                    w.mark(f"tb{g % 2}", "vE")
                    eT_done.append(w.cnt["vE"])
                w.mark(f"edbuf{up}", "tE")
                w.req("tensor", f"pv{up}")
                for j in range(32):
                    if j % 8 == 0:
                        w.wait("tensor", "vE", eT_done[j // 8])
                    w.bump(w.mm(pvb, eT[:, j * 128:(j + 1) * 128],
                                vn[:, j * 128:(j + 1) * 128],
                                start=(j == 0), stop=False), None)
                w.bump(w.mm(pvb, ones_row[:], csv_bf[:],
                            start=False, stop=True), "tE")
                for g in range(4):
                    w.mark(f"eTg{g}", "tE")
                w.wait("vector", "tE")
                ohsl = oh_ev[:, up * DH:(up + 1) * DH]
                w.wait("vector", "dA")
                w.bump(w.v("tensor_scalar", ohsl, pvb,
                           lvals[:, 6 + up:7 + up], None, ALU.mult), "vE")
                w.mark(f"pv{up}", "vE")
                w.wait("sync", "vE")
                w.bump(w.dma("sync", arawT[qb][:, h * 128:(h + 1) * 128],
                             ohsl), "dA", 16)
                if stage == 5:
                    w.wait("vector", "dO")
                    w.bump(w.v("tensor_copy", araw_f32[:], ohsl), "vE")
                    w.wait("sync", "vE")
                    w.bump(w.dma("sync", OT[qb][:, h * 128:(h + 1) * 128],
                                 araw_f32[:]), "dO", 16)

            for qb in range(NTT):
                stage_scores(qb)
                if qb > 0:
                    stage_elem(qb - 1)
                if qb > 1:
                    stage_tpv(qb - 2)
                if qb > 0:
                    stage_elem_b(qb - 1)
            stage_elem(NTT - 1)
            stage_tpv(NTT - 2)
            stage_elem_b(NTT - 1)
            stage_tpv(NTT - 1)
            w.mark(f"hd{par}", "tE")
            w.mark("csv", "tE")
        w.barrier()
        if stage == 5:
            w.wait("sync", "dO")
            return

        for kk4 in range(4):
            w.bump(w.dma("gpsimd", wo_sb[:, kk4 * 4 * D:(kk4 + 1) * 4 * D],
                         Wo[:, kk4 * 4 * D:(kk4 + 1) * 4 * D]), "dC", 16)
        x2T = x2_dram.rearrange("(n p) d -> n p d", p=128)
        x3T = x3_dram.rearrange("(n p) d -> n p d", p=128)
        w.wait("tensor", "dC")
        for i in range(NTT):
            par = i % 2
            st = stat if par == 0 else stat2
            stq6 = "dA" if par == 0 else "dT"
            art = ar_t[:, par * D:(par + 1) * D]
            x3tl = x3t[:, par * D:(par + 1) * D]
            xt = xtile[:, par * D:(par + 1) * D]
            w.req("sync", f"art{par}")
            w.bump(w.dma("sync", art, arawT[i]), f"dX{par}", 16)
            w.req("sync", f"x2t{par}")
            w.bump(w.dma("sync", xt, x2T[i]), f"dX{par}", 16)
            w.req("scalar", f"statv{par}")
            w.wait("scalar", f"dX{par}")
            w.bump(w.act(xsq[:], art, AF.Square, accum_out=st[:, 0:1]),
                   "sE")
            w.wait("vector", "sE")
            w.bump(w.v("tensor_scalar", st[:, 1:2], st[:, 0:1], 1.0 / D,
                       None, ALU.mult), "vE")
            w.bump(w.v("tensor_tensor", st[:, 2:3], st[:, 1:2],
                       consts[:, 3:4], ALU.add), "vE")
            w.wait("scalar", "vE")
            w.bump(w.act(st[:, 3:4], st[:, 2:3], AF.Ln), "sE")
            w.bump(w.act(st[:, 4:5], st[:, 3:4], AF.Exp, scale=-0.5),
                   "sE")
            w.wait("vector", "sE")
            w.bump(w.v("tensor_tensor", st[:, 5:6], st[:, 4:5], oml[:],
                       ALU.mult), "vE")
            w.mark(f"statv{par}", "vE")
            w.req("vector", "onb")
            w.bump(w.v("tensor_scalar", on_b[:], art, st[:, 5:6], None,
                       ALU.mult), "vE")
            w.mark(f"art{par}", "vE")
            w.bump(w.v("tensor_tensor", on_b[:], on_b[:], NS_rep[:],
                       ALU.mult), "vE")
            onTp = onT[:, par * D:(par + 1) * D]
            w.wait("tensor", "vE")
            w.req("tensor", f"onT{par}")
            for g in range(2):
                pb = bankb(g)
                w.req("tensor", f"wtb{g}")
                for j in range(8):
                    kk = g * 8 + j
                    w.bump(w.tr(pb[:, j * 128:(j + 1) * 128],
                                on_b[:, kk * 128:(kk + 1) * 128], ident[:]),
                           "tE" if j == 7 else None)
                w.wait("vector", "tE")
                w.bump(w.v("tensor_copy", onTp[:, g * 1024:(g + 1) * 1024],
                           pb), "vE")
                w.mark(f"wtb{g}", "vE")
            w.mark("onb", "tE")
            w.wait("tensor", "vE")
            for nci in range(4):
                bq = bank(2 + nci % 2)
                w.req("tensor", f"wob{nci % 2}")
                for kk in range(KT):
                    w.bump(w.mm(bq, onTp[:, kk * 128:(kk + 1) * 128],
                                wo_sb[:, kk * D + nci * 512:
                                      kk * D + (nci + 1) * 512],
                                start=(kk == 0), stop=(kk == KT - 1)),
                           "tE" if kk == KT - 1 else None)
                if nci == 3:
                    w.mark(f"onT{par}", "tE")
                w.wait("vector", "tE")
                sl = slice(nci * 512, (nci + 1) * 512)
                w.wait("vector", f"dX{par}")
                w.wait("vector", stq6)
                w.bump(w.v("tensor_tensor", x3tl[:, sl], bq, xt[:, sl],
                           ALU.add), "vE")
                w.mark(f"wob{nci % 2}", "vE")
                if nci == 3:
                    w.mark(f"x2t{par}", "vE")
            w.wait("gpsimd", "vE")
            w.bump(w.dma("gpsimd", x3T[i], x3tl), stq6, 16)
        w.barrier()
        if stage == 6:
            for i in range(NTT):
                par = i % 2
                w.wait("sync", "dO")
                w.bump(w.dma("sync", xtile[:, par * D:(par + 1) * D],
                             x3T[i]), "dS", 16)
                w.wait("sync", "dS")
                w.bump(w.dma("sync", OT[i], xtile[:, par * D:(par + 1) * D]),
                       "dO", 16)
            w.wait("sync", "dO")
            return

        ln_pass(x3_dram, xnT_dram[2], rs_dram[1], 2)
        w.barrier()
        ffn_pass(xnT_dram[2], rs_dram[1], x3_dram, O, True)
        w.wait("sync", "dO")
        w.barrier()

    with nc.Block() as block:
        walks = {}

        def run_walk(eng):
            wk = Walk(nc, eng, sems)
            walks[eng] = wk
            emit(wk)

        @block.tensor
        def _(te):
            run_walk("tensor")

        @block.vector
        def _(ve):
            run_walk("vector")

        @block.scalar
        def _(sc):
            run_walk("scalar")

        @block.gpsimd
        def _(gp):
            run_walk("gpsimd")

        @block.sync
        def _(sy):
            run_walk("sync")

    base = walks["tensor"].cnt
    for eng, wk in walks.items():
        assert wk.cnt == base, (eng, wk.cnt, base)
    ctx.close()
    return nc


def _tile_hid_d(w):
    return np.ascontiguousarray(
        w.reshape(HT, 128, KT, 128).transpose(0, 3, 2, 1)
        .reshape(HT, 128, KT * 128))


def _tile_din_dout(w):
    o = w.shape[1]
    return np.ascontiguousarray(
        w.reshape(KT, 128, o // 128, 128).transpose(2, 1, 0, 3)
        .reshape(o // 128, 128, KT * 128))


def _prep_inputs(L0_attn_x, L0_attn_Wq, L0_attn_Wk, L0_attn_Wv, L0_attn_Wo,
                 L0_attn_norm_scale, L0_attn_lam, L0_ffn_wq_gate, L0_ffn_s_gate,
                 L0_ffn_wq_up, L0_ffn_s_up, L0_ffn_wq_down, L0_ffn_s_down):
    x = np.ascontiguousarray(np.asarray(L0_attn_x, dtype=np.float32))
    common = dict(
        Wq=_tile_din_dout(np.asarray(L0_attn_Wq, dtype=np.float32)),
        Wk=_tile_din_dout(np.asarray(L0_attn_Wk, dtype=np.float32)),
        Wv=_tile_din_dout(np.asarray(L0_attn_Wv, dtype=np.float32)),
        Wo=np.ascontiguousarray(
            np.asarray(L0_attn_Wo, dtype=np.float32)
            .reshape(KT, 128, D).transpose(1, 0, 2).reshape(128, KT * D)),
        nsc=np.ascontiguousarray(
            np.asarray(L0_attn_norm_scale, dtype=np.float32).reshape(1, D)),
        lam=np.ascontiguousarray(
            np.asarray(L0_attn_lam, dtype=np.float32).reshape(1, NH)),
        wgT=_tile_hid_d(np.asarray(L0_ffn_wq_gate, dtype=np.int32)),
        wuT=_tile_hid_d(np.asarray(L0_ffn_wq_up, dtype=np.int32)),
        wdT=np.ascontiguousarray(
            np.asarray(L0_ffn_wq_down, dtype=np.int32)
            .reshape(8, 256, HT, 128).transpose(0, 3, 2, 1)
            .reshape(8, 128, HT * 256)),
        sg=np.ascontiguousarray(
            np.asarray(L0_ffn_s_gate, dtype=np.float32).reshape(HT, 128).T),
        su=np.ascontiguousarray(
            np.asarray(L0_ffn_s_up, dtype=np.float32).reshape(HT, 128).T),
        sd=np.ascontiguousarray(
            np.asarray(L0_ffn_s_down, dtype=np.float32).reshape(1, D)),
    )
    in_maps = []
    for c in range(8):
        b, half = c // 2, c % 2
        m = dict(common)
        m["x_in"] = np.ascontiguousarray(x[b, half * TOK:(half + 1) * TOK, :])
        in_maps.append(m)
    return in_maps


def kernel(**inputs):
    in_maps = _prep_inputs(**inputs)
    nc = build_program(stage=_STAGE)
    res = run_bass_kernel_spmd(nc, in_maps, core_ids=list(range(8)))
    out = np.zeros((4, SEQ, D), dtype=np.float32)
    for c in range(8):
        b, half = c // 2, c % 2
        out[b, half * TOK:(half + 1) * TOK, :] = np.asarray(
            res.results[c]["O"], dtype=np.float32)
    return out
